# version 47
# speedup vs baseline: 1.0673x; 1.0114x over previous
import contextlib
import numpy as np
import concourse.bass as bass
import concourse.mybir as mybir
from concourse.bass_utils import run_bass_kernel_spmd

F32 = mybir.dt.float32
BF16 = mybir.dt.bfloat16
I32 = mybir.dt.int32
ALU = mybir.AluOpType
AF = mybir.ActivationFunctionType
AX = mybir.AxisListType.X

D = 1024
NH = 16
HD = 64
NCHUNK = D // 128
NCORES = 8
SEQ = 16384
EPS_LN = 1e-5
EPS_RMS = 1e-6
EPS_ACT = 1e-5


class Prog:
    ENGINES = ("sync", "vector", "scalar", "tensor", "gpsimd")

    def __init__(self, nc):
        self.nc = nc
        self.q = {e: [] for e in self.ENGINES}
        self.cnt = {}
        self.sems = {}
        self.last_wait = {e: {} for e in self.ENGINES}
        self.stats_scg = {}
        self.es = contextlib.ExitStack()

    def sem(self, name):
        if name not in self.sems:
            self.sems[name] = self.es.enter_context(self.nc.semaphore(name))
            self.cnt[name] = 0
        return self.sems[name]

    def sb(self, name, shape, dt=F32):
        return self.es.enter_context(self.nc.sbuf_tensor(name, shape, dt))

    def ps(self, name, shape, dt=F32):
        return self.es.enter_context(self.nc.psum_tensor(name, shape, dt))

    def on(self, eng, fn):
        self.q[eng].append(fn)

    def bump(self, name, k=1):
        s = self.sem(name)
        self.cnt[name] += k
        return s, self.cnt[name]

    def wait(self, eng, name, val=None):
        if val is None:
            val = self.cnt.get(name, 0)
        self.sem(name)
        s = self.sem(name)
        lw = self.last_wait[eng]
        if lw.get(name, -1) >= val or val <= 0:
            return
        lw[name] = val
        self.on(eng, lambda e, s=s, v=val: e.wait_ge(s, v))

    def dma(self, eng, out, in_, sem_name, **kw):
        s, w = self.bump(sem_name, 16)
        self.on(eng, lambda e, out=out, in_=in_, s=s, kw=kw:
                e.dma_start(out=out, in_=in_, **kw).then_inc(s, 16))
        return w

    def op(self, eng, sem_name, fn):
        s, w = self.bump(sem_name, 1)
        self.on(eng, lambda e, s=s, fn=fn: fn(e).then_inc(s, 1))
        return w

    def run(self, block):
        for eng_name in self.ENGINES:
            fns = self.q[eng_name]

            def body(e, fns=fns):
                for f in fns:
                    f(e)
            getattr(block, eng_name)(body)


def load_weight_F(p, src_ap, dram_stage, sbuf_dst, sem_name,
                  cast_after=(), store_after=(), eng="scalar"):
    for nm, v in cast_after:
        p.wait("gpsimd", nm, v)
    p.dma("gpsimd", dram_stage[:, :], src_ap[:, :], sem_name + "c")
    for nm, v in store_after:
        p.wait(eng, nm, v)
    p.wait(eng, sem_name + "c", p.cnt[sem_name + "c"])
    for c in range(NCHUNK):
        p.dma(eng, sbuf_dst[:, c * D:(c + 1) * D],
              dram_stage[:, c * 128:(c + 1) * 128], sem_name, transpose=True)
    return p.cnt[sem_name]


def weight_cast_chunk(p, src_ap, dram_stage, sem_name, c, cast_after=()):
    for nm, v in cast_after:
        p.wait("gpsimd", nm, v)
    p.dma("gpsimd", dram_stage[c * 128:(c + 1) * 128, :],
          src_ap[c * 128:(c + 1) * 128, :], sem_name + "c")


def weight_transposes(p, dram_stage, sbuf_dst, sem_name, store_after=()):
    for nm, v in store_after:
        p.wait("scalar", nm, v)
    p.wait("scalar", sem_name + "c", p.cnt[sem_name + "c"])
    for c in range(NCHUNK):
        p.dma("scalar", sbuf_dst[:, c * D:(c + 1) * D],
              dram_stage[:, c * 128:(c + 1) * 128], sem_name, transpose=True)
    return p.cnt[sem_name]


def emit_ln_stats(p, xs_t, stats_t, bnrow):
    p.op("vector", "dv", lambda e: e.bn_stats(bnrow[:, 0:6], xs_t[:, 0:512]))
    w2 = p.op("vector", "dv", lambda e: e.bn_stats(bnrow[:, 6:12], xs_t[:, 512:1024]))
    p.wait("vector", "dv", w2)
    w3 = p.op("vector", "dv", lambda e: e.bn_aggr(stats_t[:, 0:2], bnrow[:, :]))
    p.wait("vector", "dv", w3)
    w3b = p.op("vector", "dv", lambda e: e.tensor_scalar(stats_t[:, 6:7], stats_t[:, 1:2], EPS_LN, None, ALU.add))
    p.wait("vector", "dv", w3b)
    p.wait("scalar", "dv", w3b)
    w4 = p.op("scalar", "sc", lambda e: e.activation(stats_t[:, 6:7], stats_t[:, 6:7], AF.Sqrt))
    p.wait("vector", "sc", w4)
    w5 = p.op("vector", "dv", lambda e: e.reciprocal(stats_t[:, 2:3], stats_t[:, 6:7]))
    p.wait("vector", "dv", w5)
    return w5


def emit_ffn_quant(p, xs_t, stats_t, yt, it, out_bf_t, wsb128):
    w6 = p.op("vector", "dv", lambda e: e.tensor_scalar(yt[:, :], xs_t[:, :], stats_t[:, 0:1], None, ALU.subtract))
    p.wait("vector", "dv", w6)
    w7 = p.op("vector", "dv", lambda e: e.reduce_max(stats_t[:, 3:4], yt[:, :], AX, apply_absolute_value=True))
    p.wait("vector", "dv", w7)
    w8 = p.op("vector", "dv", lambda e: e.tensor_scalar(stats_t[:, 7:8], stats_t[:, 2:3], stats_t[:, 2:3], None, ALU.mult))
    p.wait("vector", "dv", w8)
    w9 = p.op("vector", "dv", lambda e: e.tensor_scalar(stats_t[:, 7:8], stats_t[:, 1:2], stats_t[:, 7:8], EPS_RMS, ALU.mult, ALU.add))
    p.wait("vector", "dv", w9)
    p.wait("scalar", "dv", w9)
    w10 = p.op("scalar", "sc", lambda e: e.activation(stats_t[:, 7:8], stats_t[:, 7:8], AF.Sqrt))
    p.wait("vector", "sc", w10)
    w11 = p.op("vector", "dv", lambda e: e.reciprocal(stats_t[:, 7:8], stats_t[:, 7:8]))
    p.wait("vector", "dv", w11)
    w12 = p.op("vector", "dv", lambda e: e.tensor_scalar(stats_t[:, 7:8], stats_t[:, 7:8], stats_t[:, 2:3], None, ALU.mult))
    p.wait("vector", "dv", w12)
    w13 = p.op("vector", "dv", lambda e: e.tensor_scalar(stats_t[:, 3:4], stats_t[:, 3:4], stats_t[:, 7:8], EPS_ACT, ALU.mult, ALU.max))
    p.wait("vector", "dv", w13)
    w14 = p.op("vector", "dv", lambda e: e.reciprocal(stats_t[:, 4:5], stats_t[:, 3:4]))
    p.wait("vector", "dv", w14)
    w15 = p.op("vector", "dv", lambda e: e.tensor_scalar(stats_t[:, 4:5], stats_t[:, 4:5], 127.0, stats_t[:, 7:8], ALU.mult, ALU.mult))
    w16 = p.op("vector", "dv", lambda e: e.tensor_scalar(stats_t[:, 5:6], stats_t[:, 3:4], wsb128[:, 0:1], 0.5 / 127.0, ALU.mult, ALU.mult))
    p.wait("vector", "dv", max(w15, w16))
    w17 = p.op("vector", "dv", lambda e: e.tensor_scalar(it[:, :], yt[:, :], stats_t[:, 4:5], None, ALU.mult))
    p.wait("vector", "dv", w17)
    w18 = p.op("vector", "dv", lambda e: e.tensor_copy(out_bf_t[:, :], it[:, :]))
    return w18


class MMLoop:

    def __init__(self, p, rots):
        self.p = p
        self.rots = rots
        self.guard = [None] * len(rots)

    def pre(self, i):
        g = self.guard[i % len(self.rots)]
        if g:
            self.p.wait("tensor", g[0], g[1])

    def set_guard(self, i, sem_name, val):
        self.guard[i % len(self.rots)] = (sem_name, val)


def matmul_tiles(p, mml, NT, T, lhs_buf, wt_buf, bias_row, evac, extra_pe_wait=None):
    for t in range(NT):
        mml.pre(t)
        rot = mml.rots[t % len(mml.rots)]
        for half in range(2):
            o0 = half * 512
            pb = rot[half]
            for c in range(NCHUNK):
                s, _ = p.bump("pe", 1)
                p.on("tensor", lambda e, t=t, c=c, o0=o0, pb=pb, s=s, first=(c == 0),
                     last=(c == NCHUNK - 1 and bias_row is None), lhs=lhs_buf, wt=wt_buf:
                     e.matmul(pb[:, :], lhs[:, c * T + t * 128:c * T + (t + 1) * 128],
                              wt[:, c * D + o0:c * D + o0 + 512],
                              start=first, stop=last).then_inc(s, 1))
            if bias_row is not None:
                s, _ = p.bump("pe", 1)
                p.on("tensor", lambda e, o0=o0, pb=pb, s=s, br=bias_row:
                     e.matmul(pb[:, :], mml.onesr[:, :], br[:, o0:o0 + 512],
                              start=False, stop=True).then_inc(s, 1))
        g = evac(t, rot, p.cnt["pe"])
        mml.set_guard(t, g[0], g[1])


def build_l1(T):
    nc = bass.Bass("TRN2", target_bir_lowering=False, debug=False)
    NT = T // 128
    xin = nc.dram_tensor("x", [T, D], F32, kind="ExternalInput").ap()
    Wq = nc.dram_tensor("Wq", [D, D], F32, kind="ExternalInput").ap()
    Wk = nc.dram_tensor("Wk", [D, D], F32, kind="ExternalInput").ap()
    Wv = nc.dram_tensor("Wv", [D, D], F32, kind="ExternalInput").ap()
    bqi = nc.dram_tensor("bq", [1, D], F32, kind="ExternalInput").ap()
    bki = nc.dram_tensor("bk", [1, D], F32, kind="ExternalInput").ap()
    bvi = nc.dram_tensor("bv", [1, D], F32, kind="ExternalInput").ap()
    w8i = nc.dram_tensor("w8", [D, D], I32, kind="ExternalInput").ap()
    wsi = nc.dram_tensor("ws", [1, 1], F32, kind="ExternalInput").ap()
    fbi = nc.dram_tensor("fb", [1, D], F32, kind="ExternalInput").ap()
    x1o = nc.dram_tensor("x1", [T, D], F32, kind="ExternalOutput").ap()
    eqo = nc.dram_tensor("eqn", [T, D], BF16, kind="ExternalOutput").ap()
    uzo = nc.dram_tensor("uz", [128, 8 * 65], F32, kind="ExternalOutput").ap()
    stage = nc.dram_tensor("stage", [T, D], BF16).ap()
    wstage = nc.dram_tensor("wstage", [D, D], BF16).ap()
    wstage2 = nc.dram_tensor("wstage2", [D, D], BF16).ap()

    p = Prog(nc)
    xs = p.sb("xs", [128, NT * D])
    A1 = p.sb("A1", [128, NT * D], BF16)
    A2 = p.sb("A2", [128, NCHUNK * T], BF16)
    A3 = p.sb("A3", [128, NT * D], BF16)
    WT = p.sb("WT", [128, NCHUNK * D], BF16)
    stats = p.sb("stats", [128, NT * 8])
    bnrow = p.sb("bnrow", [128, 12])
    brow = p.sb("brow", [1, 3 * D], BF16)
    onesr = p.sb("onesr", [1, 128], BF16)
    ones32 = p.sb("ones32", [1, 128])
    wsb = p.sb("wsb", [1, 1])
    wsb128 = p.sb("wsb128", [128, 1])
    fbrow = p.sb("fbrow", [1, D], BF16)
    bb = p.sb("bb", [128, D], BF16)
    vtile = p.sb("vtile", [128, 2 * 1040], BF16)
    uzsb = p.sb("uzsb", [128, 8 * 65])
    itile = p.sb("itile", [128, D], I32)
    ytile = p.sb("ytile", [128, D])
    sr = p.sb("sr", [128, NT * 16])
    mm0 = p.ps("mm0", [128, 512])
    mm1 = p.ps("mm1", [128, 512])
    mm2 = p.ps("mm2", [128, 512])
    mm3 = p.ps("mm3", [128, 512])
    uzpe0 = p.ps("uzpe0", [64, 260])
    uzpo0 = p.ps("uzpo0", [64, 260])
    uzpe1 = p.ps("uzpe1", [64, 260])
    uzpo1 = p.ps("uzpo1", [64, 260])
    uzp = {(0, 0): uzpe0, (0, 1): uzpo0, (1, 0): uzpe1, (1, 1): uzpo1}
    mml = MMLoop(p, [(mm0, mm1), (mm2, mm3)])
    mml.onesr = onesr
    stage_t = stage.rearrange("(a p) d -> a p d", p=128)

    wx = p.dma("sync", xs[:, :].rearrange("p (a d) -> p a d", a=NT),
               xin.rearrange("(a p) d -> p a d", p=128), "xin")
    p.dma("gpsimd", brow[:, 0:D], bqi[:, :], "rowsc")
    p.dma("gpsimd", brow[:, D:2 * D], bki[:, :], "rowsc")
    p.dma("gpsimd", brow[:, 2 * D:3 * D], bvi[:, :], "rowsc")
    wbr = p.dma("gpsimd", fbrow[:, :], fbi[:, :], "rowsc")
    wrows = p.dma("sync", wsb[:, :], wsi[:, :], "rows")
    p.wait("vector", "rows", wrows)
    p.wait("vector", "rowsc", wbr)
    wfb = 0
    wones = p.op("vector", "dv", lambda e: e.memset(onesr[:, :], 1.0))
    wones32 = p.op("vector", "dv", lambda e: e.memset(ones32[:, :], 1.0))
    p.wait("tensor", "dv", wones32)
    s, wpe_ws = p.bump("pe", 1)
    p.on("tensor", lambda e, s=s: e.matmul(mm2[:, 0:1], ones32[:, :], wsb[:, :], start=True, stop=True).then_inc(s, 1))
    p.wait("vector", "pe", wpe_ws)
    wwsb = p.op("vector", "dv", lambda e: e.tensor_copy(wsb128[:, :], mm2[:, 0:1]))
    mml.set_guard(1, "dv", wwsb)
    p.wait("tensor", "dv", max(wones, wfb))
    s, _ = p.bump("pe", 1)
    p.on("tensor", lambda e, s=s: e.matmul(mm0[:, :], onesr[:, :], fbrow[:, 0:512], start=True, stop=True).then_inc(s, 1))
    s, wpe_bb = p.bump("pe", 1)
    p.on("tensor", lambda e, s=s: e.matmul(mm1[:, :], onesr[:, :], fbrow[:, 512:1024], start=True, stop=True).then_inc(s, 1))
    p.wait("vector", "pe", wpe_bb)
    p.op("vector", "dv", lambda e: e.tensor_scalar(bb[:, 0:512], mm0[:, :], 0.5, None, ALU.mult))
    wbb = p.op("vector", "dv", lambda e: e.tensor_scalar(bb[:, 512:1024], mm1[:, :], 0.5, None, ALU.mult))
    mml.set_guard(0, "dv", wbb)

    w_w8 = load_weight_F(p, w8i, wstage, WT, "w8w")

    p.wait("vector", "xin", wx)
    p.wait("vector", "dv", wwsb)
    for t in range(NT):
        xt = xs[:, t * D:(t + 1) * D]
        stt = stats[:, t * 8:(t + 1) * 8]
        emit_ln_stats(p, xt, stt, bnrow)
        yt = ytile[:, 0:D]
        it = itile[:, 0:D]
        w18 = emit_ffn_quant(p, xt, stt, yt, it, A3[:, t * D:(t + 1) * D], wsb128)
        p.wait("sync", "dv", w18)
        p.dma("sync", stage_t[t], A3[:, t * D:(t + 1) * D], "st")
    wst1 = p.cnt["st"]
    p.wait("scalar", "st", wst1)
    for c in range(NCHUNK):
        p.dma("scalar", A2[:, c * T:(c + 1) * T], stage[:, c * 128:(c + 1) * 128], "tp", transpose=True)
    wtp1 = p.cnt["tp"]
    p.wait("tensor", "tp", wtp1)
    p.wait("tensor", "w8w", w_w8)

    def evac_ffn(t, rot, wpe):
        stt_c = stats[:, t * 8 + 5:t * 8 + 6]
        p.wait("vector", "pe", wpe)
        for half in range(2):
            o0 = half * 512
            wv = p.op("vector", "dv", lambda e, t=t, o0=o0, pb=rot[half], stt_c=stt_c:
                      e.scalar_tensor_tensor(xs[:, t * D + o0:t * D + o0 + 512], pb[:, :], stt_c,
                                             xs[:, t * D + o0:t * D + o0 + 512], ALU.mult, ALU.add))
        return ("dv", wv)

    matmul_tiles(p, mml, NT, T, A2, WT, None, evac_ffn)
    wff1 = p.cnt["dv"]
    p.wait("vector", "dv", max(wff1, wbb))
    for t in range(NT):
        wx1 = p.op("vector", "dv", lambda e, t=t: e.tensor_tensor(
            xs[:, t * D:(t + 1) * D], xs[:, t * D:(t + 1) * D], bb[:, :], ALU.add))
    p.wait("sync", "dv", wx1)
    p.dma("sync", x1o.rearrange("(a p) d -> p a d", p=128),
          xs[:, :].rearrange("p (a d) -> p a d", a=NT), "st")

    p.wait("vector", "st", wst1)
    p.wait("vector", "dv", wx1)
    for t in range(NT):
        xt = xs[:, t * D:(t + 1) * D]
        stt = stats[:, t * 8:(t + 1) * 8]
        emit_ln_stats(p, xt, stt, bnrow)
        w6 = p.op("vector", "dv", lambda e, xt=xt, stt=stt, t=t:
                  e.tensor_scalar(A3[:, t * D:(t + 1) * D], xt[:, :], stt[:, 0:1], stt[:, 2:3],
                                  ALU.subtract, ALU.mult))
        p.wait("sync", "dv", w6)
        p.wait("sync", "tp", wtp1)
        p.dma("sync", stage_t[t], A3[:, t * D:(t + 1) * D], "st")
    wst2 = p.cnt["st"]
    p.wait("scalar", "pe", p.cnt["pe"])
    p.wait("scalar", "st", wst2)
    for c in range(NCHUNK):
        p.dma("scalar", A2[:, c * T:(c + 1) * T], stage[:, c * 128:(c + 1) * 128], "tp", transpose=True)
    wtp2 = p.cnt["tp"]

    w_wk = load_weight_F(p, Wk, wstage2, WT, "wkw",
                         store_after=[("pe", p.cnt["pe"])])
    p.wait("tensor", "tp", wtp2)
    p.wait("tensor", "wkw", w_wk)
    p.wait("tensor", "rowsc", wbr)
    ek_marks = {}

    def evac_k(t, rot, wpe):
        p.wait("scalar", "pe", wpe)
        for half in range(2):
            o0 = half * 512
            wsc = p.op("scalar", "sc", lambda e, t=t, o0=o0, pb=rot[half]:
                       e.activation(A1[:, t * D + o0:t * D + o0 + 512], pb[:, :], AF.Exp))
        ek_marks[t] = wsc
        return ("sc", wsc)

    matmul_tiles(p, mml, NT, T, A2, WT, brow[:, D:2 * D], evac_k)
    wek = p.cnt["sc"]

    w_wv = load_weight_F(p, Wv, wstage, WT, "wvw",
                         cast_after=[("w8w", w_w8)],
                         store_after=[("pe", p.cnt["pe"])])
    memv = p.op("vector", "dv", lambda e: e.memset(vtile[:, :], 1.0))
    p.wait("tensor", "wvw", w_wv)
    p.wait("tensor", "dv", memv)
    p.wait("tensor", "sc", wek)

    vt_guard = [None, None]

    def evac_v(t, rot, wpe):
        vt = vtile[:, (t % 2) * 1040:(t % 2) * 1040 + 1040]
        if vt_guard[t % 2]:
            p.wait("vector", "pe", vt_guard[t % 2])
        p.wait("vector", "pe", wpe)
        for half in range(2):
            o0 = half * 512
            wv = p.op("vector", "dv", lambda e, vt=vt, o0=o0, pb=rot[half]:
                      e.tensor_copy(
                          vt[:, :].rearrange("p (h x) -> p h x", x=65)[:, o0 // 64:o0 // 64 + 8, 0:64],
                          pb[:, :].rearrange("p (h x) -> p h x", x=64)))
        p.wait("tensor", "dv", wv)
        for h in range(NH):
            j = h // 2
            pu = uzp[(j // 4, h % 2)]
            col = (j % 4) * 65
            first = (t == 0) and j % 4 == 0
            last = (t == NT - 1) and j % 4 == 3
            s, wpe_u = p.bump("pe", 1)
            p.on("tensor", lambda e, t=t, h=h, pu=pu, col=col, s=s, first=first, last=last:
                 e.matmul(pu[:, col:col + 65],
                          A1[:, t * D + h * 64:t * D + h * 64 + 64],
                          vtile[:, (t % 2) * 1040 + h * 65:(t % 2) * 1040 + h * 65 + 65],
                          start=first, stop=last).then_inc(s, 1))
        vt_guard[t % 2] = wpe_u
        return ("dv", wv)

    matmul_tiles(p, mml, NT, T, A2, WT, brow[:, 2 * D:3 * D], evac_v)
    wuz = p.cnt["pe"]
    p.wait("vector", "pe", wuz)
    p.op("vector", "dv", lambda e: e.tensor_copy(uzsb[0:64, 0:260], uzpe0[:, :]))
    p.op("vector", "dv", lambda e: e.tensor_copy(uzsb[64:128, 0:260], uzpo0[:, :]))
    p.op("vector", "dv", lambda e: e.tensor_copy(uzsb[0:64, 260:520], uzpe1[:, :]))
    wuz2 = p.op("vector", "dv", lambda e: e.tensor_copy(uzsb[64:128, 260:520], uzpo1[:, :]))
    p.wait("sync", "dv", wuz2)
    p.dma("sync", uzo[:, :], uzsb[:, :], "st")

    w_wq = load_weight_F(p, Wq, wstage2, WT, "wqw",
                         cast_after=[("wkw", w_wk)],
                         store_after=[("pe", p.cnt["pe"])])
    p.wait("tensor", "wqw", w_wq)
    p.wait("scalar", "st", wst2)
    p.wait("vector", "pe", wuz)

    def evac_q(t, rot, wpe):
        p.wait("scalar", "pe", wpe)
        for half in range(2):
            o0 = half * 512
            wsc = p.op("scalar", "sc", lambda e, t=t, o0=o0, pb=rot[half]:
                       e.activation(A3[:, t * D + o0:t * D + o0 + 512], pb[:, :], AF.Exp))
        p.wait("vector", "sc", wsc)
        w1 = p.op("vector", "dv", lambda e, t=t:
                  e.tensor_reduce(sr[:, t * 16:(t + 1) * 16],
                                  A3[:, t * D:(t + 1) * D].rearrange("p (h d) -> p h d", h=NH),
                                  AX, ALU.add))
        p.wait("vector", "dv", w1)
        w2 = p.op("vector", "dv", lambda e, t=t:
                  e.reciprocal(sr[:, t * 16:(t + 1) * 16], sr[:, t * 16:(t + 1) * 16]))
        p.wait("vector", "dv", w2)
        w3 = p.op("vector", "dv", lambda e, t=t:
                  e.tensor_tensor(A1[:, t * D:(t + 1) * D].rearrange("p (h d) -> p h d", h=NH),
                                  A3[:, t * D:(t + 1) * D].rearrange("p (h d) -> p h d", h=NH),
                                  sr[:, t * 16:(t + 1) * 16].rearrange("p (h o) -> p h o", o=1).broadcast_to([128, NH, HD]),
                                  ALU.mult))
        p.wait("sync", "dv", w3)
        p.dma("sync", eqo.rearrange("(a p) d -> a p d", p=128)[t], A1[:, t * D:(t + 1) * D], "st")
        return ("dv", w3)

    matmul_tiles(p, mml, NT, T, A2, WT, brow[:, 0:D], evac_q)

    p.wait("sync", "st", p.cnt["st"])
    for nm in ("xin", "rows", "rowsc", "w8w", "wkw", "wvw", "wqw", "tp"):
        p.wait("sync", nm, p.cnt[nm])

    with p.es, nc.Block() as block:
        p.run(block)
    return nc


def build_l2(T, cut=None):
    nc = bass.Bass("TRN2", target_bir_lowering=False, debug=False)
    NT = T // 128
    x1i = nc.dram_tensor("x1", [T, D], F32, kind="ExternalInput").ap()
    eqi = nc.dram_tensor("eqn", [T, D], BF16, kind="ExternalInput").ap()
    uzi = nc.dram_tensor("uzall", [8 * 128, 8 * 65], F32, kind="ExternalInput").ap()
    Wo = nc.dram_tensor("Wo", [D, D], F32, kind="ExternalInput").ap()
    boi = nc.dram_tensor("bo", [1, D], F32, kind="ExternalInput").ap()
    gi = nc.dram_tensor("g", [1, D], F32, kind="ExternalInput").ap()
    bi = nc.dram_tensor("b", [1, D], F32, kind="ExternalInput").ap()
    w8i = nc.dram_tensor("w8", [D, D], I32, kind="ExternalInput").ap()
    wsi = nc.dram_tensor("ws", [1, 1], F32, kind="ExternalInput").ap()
    fbi = nc.dram_tensor("fb", [1, D], F32, kind="ExternalInput").ap()
    yo = nc.dram_tensor("y", [T, D], F32, kind="ExternalOutput").ap()
    stage = nc.dram_tensor("stage", [T, D], BF16).ap()
    wstage = nc.dram_tensor("wstage", [D, D], BF16).ap()
    wstage2 = nc.dram_tensor("wstage2", [D, D], BF16).ap()

    p = Prog(nc)
    xs = p.sb("xs", [128, NT * D])
    A1 = p.sb("A1", [128, NCHUNK * T], BF16)
    A2 = p.sb("A2", [128, NT * D], BF16)
    A3 = p.sb("A3", [128, NCHUNK * T], BF16)
    WT = p.sb("WT", [128, NCHUNK * D], BF16)
    stats = p.sb("stats", [128, NT * 8])
    bnrow = p.sb("bnrow", [128, 12])
    uzbuf = p.sb("uzbuf", [128, 2 * 520])
    uzsum = p.sb("uzsum", [128, 520])
    state = p.sb("state", [128, 8 * 128], BF16)
    zr = p.sb("zr", [128, 8])
    gF = p.sb("gF", [128, NCHUNK])
    bF = p.sb("bF", [128, NCHUNK], BF16)
    bF32 = p.sb("bF32", [128, NCHUNK])
    borow32 = p.sb("borow32", [1, D])
    boeff = p.sb("boeff", [1, D], BF16)
    fbrow = p.sb("fbrow", [1, D], BF16)
    bb = p.sb("bb", [128, D], BF16)
    onesr = p.sb("onesr", [1, 128], BF16)
    ones32 = p.sb("ones32", [1, 128])
    wsb = p.sb("wsb", [1, 1])
    wsb128 = p.sb("wsb128", [128, 1])
    itile = p.sb("itile", [128, D], I32)
    ytile = p.sb("ytile", [128, D])
    mm0 = p.ps("mm0", [128, 512])
    mm1 = p.ps("mm1", [128, 512])
    mm2 = p.ps("mm2", [128, 512])
    mm3 = p.ps("mm3", [128, 512])
    wsp = p.ps("wsp", [128, 1])
    bop0 = p.ps("bop0", [1, 512])
    bop1 = p.ps("bop1", [1, 512])
    mml = MMLoop(p, [(mm0, mm1), (mm2, mm3)])
    mml.onesr = onesr
    stage_t = stage.rearrange("(a p) d -> a p d", p=128)


    def early_out():
        wyx = p.op("vector", "dv", lambda e: e.tensor_copy(xs[:, 0:D], xs[:, 0:D]))
        p.wait("sync", "dv", wyx)
        p.dma("sync", yo.rearrange("(a p) d -> p a d", p=128),
              xs[:, :].rearrange("p (a d) -> p a d", a=NT), "st")
        p.wait("sync", "st", p.cnt["st"])
        with p.es, nc.Block() as block:
            p.run(block)
        return nc
    wx = p.dma("sync", xs[:, :].rearrange("p (a d) -> p a d", a=NT),
               x1i.rearrange("(a p) d -> p a d", p=128), "xin")
    uzl_marks = []
    for r in range(2):
        uzl_marks.append(p.dma("sync", uzbuf[:, (r % 2) * 520:(r % 2) * 520 + 520],
                               uzi[r * 128:(r + 1) * 128, :], "uzl"))
    p.dma("sync", borow32[:, :], boi[:, :], "rows")
    wfbc = p.dma("gpsimd", fbrow[:, :], fbi[:, :], "rowsc")
    p.dma("sync", wsb[:, :], wsi[:, :], "rows")
    p.dma("sync", gF[:, :], gi.rearrange("o (c pp) -> pp (o c)", pp=128), "rows",
          allow_slow_non_contiguous=True)
    wrows = p.dma("sync", bF32[:, :], bi.rearrange("o (c pp) -> pp (o c)", pp=128), "rows",
                  allow_slow_non_contiguous=True)
    for c in range(NCHUNK):
        p.dma("scalar", A1[:, c * T:(c + 1) * T], eqi[:, c * 128:(c + 1) * 128], "tp", transpose=True)
    weqT = p.cnt["tp"]

    p.wait("vector", "rows", wrows)
    wones = p.op("vector", "dv", lambda e: e.memset(onesr[:, :], 1.0))
    wones32 = p.op("vector", "dv", lambda e: e.memset(ones32[:, :], 1.0))
    wbF = p.op("vector", "dv", lambda e: e.tensor_copy(bF[:, :], bF32[:, :]))
    p.wait("tensor", "dv", wones32)
    s, wpe_ws = p.bump("pe", 1)
    p.on("tensor", lambda e, s=s: e.matmul(wsp[:, :], ones32[:, :], wsb[:, :], start=True, stop=True).then_inc(s, 1))
    p.wait("vector", "pe", wpe_ws)
    wwsb = p.op("vector", "dv", lambda e: e.tensor_copy(wsb128[:, :], wsp[:, :]))

    w_wo = load_weight_F(p, Wo, wstage, WT, "wow")
    p.wait("tensor", "wow", w_wo)
    p.wait("tensor", "dv", wbF)
    for c in range(NCHUNK):
        s, _ = p.bump("pe", 1)
        p.on("tensor", lambda e, c=c, s=s: e.matmul(bop0[:, :], bF[:, c:c + 1], WT[:, c * D:c * D + 512],
                                                    start=(c == 0), stop=(c == NCHUNK - 1)).then_inc(s, 1))
        s, wboe = p.bump("pe", 1)
        p.on("tensor", lambda e, c=c, s=s: e.matmul(bop1[:, :], bF[:, c:c + 1], WT[:, c * D + 512:c * D + 1024],
                                                    start=(c == 0), stop=(c == NCHUNK - 1)).then_inc(s, 1))
    p.wait("vector", "pe", wboe)
    w1 = p.op("vector", "dv", lambda e: e.tensor_tensor(boeff[:, 0:512], bop0[:, :], borow32[:, 0:512], ALU.add))
    wboeff = p.op("vector", "dv", lambda e: e.tensor_tensor(boeff[:, 512:1024], bop1[:, :], borow32[:, 512:1024], ALU.add))
    for c in range(NCHUNK):
        wgf = p.op("vector", "dv", lambda e, c=c: e.tensor_scalar(
            WT[:, c * D:(c + 1) * D], WT[:, c * D:(c + 1) * D], gF[:, c:c + 1], None, ALU.mult))
    wgfold = wgf

    p.wait("tensor", "dv", wones)
    p.wait("tensor", "rowsc", wfbc)
    s, _ = p.bump("pe", 1)
    p.on("tensor", lambda e, s=s: e.matmul(mm0[:, :], onesr[:, :], fbrow[:, 0:512], start=True, stop=True).then_inc(s, 1))
    s, wpe_bb = p.bump("pe", 1)
    p.on("tensor", lambda e, s=s: e.matmul(mm1[:, :], onesr[:, :], fbrow[:, 512:1024], start=True, stop=True).then_inc(s, 1))
    p.wait("vector", "pe", wpe_bb)
    p.op("vector", "dv", lambda e: e.tensor_scalar(bb[:, 0:512], mm0[:, :], 0.5, None, ALU.mult))
    wbb = p.op("vector", "dv", lambda e: e.tensor_scalar(bb[:, 512:1024], mm1[:, :], 0.5, None, ALU.mult))
    mml.set_guard(0, "dv", wbb)

    p.wait("vector", "uzl", p.cnt["uzl"])
    w = p.op("vector", "dv", lambda e: e.tensor_copy(uzsum[:, :], uzbuf[:, 0:520]))
    for r in range(1, 8):
        if r + 1 < 8:
            p.wait("sync", "dv", w)
            p.dma("sync", uzbuf[:, ((r + 1) % 2) * 520:((r + 1) % 2) * 520 + 520],
                  uzi[(r + 1) * 128:(r + 2) * 128, :], "uzl")
        p.wait("vector", "uzl", p.cnt["uzl"])
        p.wait("vector", "dv", w)
        w = p.op("vector", "dv", lambda e, r=r: e.tensor_tensor(uzsum[:, :], uzsum[:, :],
                                                                uzbuf[:, (r % 2) * 520:(r % 2) * 520 + 520], ALU.add))
    p.wait("vector", "dv", w)
    wzr = p.op("vector", "dv", lambda e: e.reciprocal(
        zr[:, :].rearrange("p (j o) -> p j o", o=1),
        uzsum[:, :].rearrange("p (j x) -> p j x", x=65)[:, :, 64:65]))
    p.wait("vector", "dv", wzr)
    wms = p.op("vector", "dv", lambda e: e.memset(state[:, :], 0.0))
    p.wait("vector", "dv", wms)
    for j in range(8):
        p.op("vector", "dv", lambda e, j=j: e.tensor_scalar(
            state[0:64, j * 128:j * 128 + 64], uzsum[0:64, j * 65:j * 65 + 64], zr[0:64, j:j + 1], None, ALU.mult))
        wstate = p.op("vector", "dv", lambda e, j=j: e.tensor_scalar(
            state[64:128, j * 128 + 64:j * 128 + 128], uzsum[64:128, j * 65:j * 65 + 64], zr[64:128, j:j + 1], None, ALU.mult))

    if cut == "state":
        p.wait("vector", "dv", wstate)
        p.wait("vector", "xin", wx)
        p.wait("vector", "tp", weqT)
        return early_out()
    p.wait("tensor", "dv", wstate)
    p.wait("tensor", "tp", weqT)
    a_marks = {}
    for t in range(NT):
        mml.pre(t)
        rot = mml.rots[t % 2]
        for j in range(8):
            pb = rot[j // 4]
            col = (j * 128) % 512
            s, _ = p.bump("pe", 1)
            p.on("tensor", lambda e, t=t, j=j, pb=pb, col=col, s=s:
                 e.matmul(pb[:, col:col + 128],
                          A1[:, j * T + t * 128:j * T + (t + 1) * 128],
                          state[:, j * 128:(j + 1) * 128],
                          start=True, stop=True).then_inc(s, 1))
        wpe_a = p.cnt["pe"]
        p.wait("scalar", "pe", wpe_a)
        for half in range(2):
            o0 = half * 512
            wsc = p.op("scalar", "sc", lambda e, t=t, o0=o0, pb=rot[half]:
                       e.activation(A2[:, t * D + o0:t * D + o0 + 512], pb[:, :], AF.Copy))
        a_marks[t] = wsc
        mml.set_guard(t, "sc", wsc)

    if cut == "anum":
        p.wait("vector", "sc", a_marks[NT - 1])
        p.wait("vector", "xin", wx)
        return early_out()
    for t in range(NT):
        at = A2[:, t * D:(t + 1) * D]
        stt = stats[:, t * 8:(t + 1) * 8]
        p.wait("vector", "sc", a_marks[t])
        emit_ln_stats(p, at, stt, bnrow)
        w6 = p.op("vector", "dv", lambda e, at=at, stt=stt:
                  e.tensor_scalar(at[:, :], at[:, :], stt[:, 0:1], stt[:, 2:3], ALU.subtract, ALU.mult))
        p.wait("sync", "dv", w6)
        p.dma("sync", stage_t[t], at, "st")
    waln = p.cnt["st"]
    p.wait("scalar", "st", waln)
    for c in range(NCHUNK):
        p.dma("scalar", A3[:, c * T:(c + 1) * T], stage[:, c * 128:(c + 1) * 128], "tp", transpose=True)
    walnT = p.cnt["tp"]

    if cut == "aln":
        p.wait("vector", "tp", walnT)
        p.wait("vector", "xin", wx)
        return early_out()
    p.wait("tensor", "tp", walnT)
    p.wait("tensor", "dv", max(wgfold, wboeff))

    def evac_wo(t, rot, wpe):
        p.wait("vector", "pe", wpe)
        for half in range(2):
            o0 = half * 512
            wv = p.op("vector", "dv", lambda e, t=t, o0=o0, pb=rot[half]:
                      e.tensor_tensor(xs[:, t * D + o0:t * D + o0 + 512],
                                      xs[:, t * D + o0:t * D + o0 + 512], pb[:, :], ALU.add))
        return ("dv", wv)

    p.wait("vector", "xin", wx)
    matmul_tiles(p, mml, NT, T, A3, WT, boeff, evac_wo)
    wx2 = p.cnt["dv"]

    if cut == "wo":
        p.wait("vector", "dv", wx2)
        return early_out()
    w_w8 = load_weight_F(p, w8i, wstage2, WT, "w8w",
                         store_after=[("pe", p.cnt["pe"])])
    p.wait("vector", "dv", max(wx2, wwsb))
    p.wait("vector", "st", waln)
    for t in range(NT):
        xt = xs[:, t * D:(t + 1) * D]
        stt = stats[:, t * 8:(t + 1) * 8]
        emit_ln_stats(p, xt, stt, bnrow)
        yt = ytile[:, 0:D]
        it = itile[:, 0:D]
        w18 = emit_ffn_quant(p, xt, stt, yt, it, A2[:, t * D:(t + 1) * D], wsb128)
        p.wait("sync", "dv", w18)
        p.wait("sync", "tp", walnT)
        p.dma("sync", stage_t[t], A2[:, t * D:(t + 1) * D], "st")
    wst2 = p.cnt["st"]
    p.wait("scalar", "pe", p.cnt["pe"])
    p.wait("scalar", "st", wst2)
    for c in range(NCHUNK):
        p.dma("scalar", A3[:, c * T:(c + 1) * T], stage[:, c * 128:(c + 1) * 128], "tp", transpose=True)
    wt2T = p.cnt["tp"]
    p.wait("tensor", "tp", wt2T)
    p.wait("tensor", "w8w", w_w8)

    def evac_ffn2(t, rot, wpe):
        stt_c = stats[:, t * 8 + 5:t * 8 + 6]
        p.wait("vector", "pe", wpe)
        for half in range(2):
            o0 = half * 512
            wv = p.op("vector", "dv", lambda e, t=t, o0=o0, pb=rot[half], stt_c=stt_c:
                      e.scalar_tensor_tensor(xs[:, t * D + o0:t * D + o0 + 512], pb[:, :], stt_c,
                                             xs[:, t * D + o0:t * D + o0 + 512], ALU.mult, ALU.add))
        return ("dv", wv)

    matmul_tiles(p, mml, NT, T, A3, WT, None, evac_ffn2)
    wff2 = p.cnt["dv"]
    p.wait("vector", "dv", max(wff2, wbb))
    for t in range(NT):
        wy = p.op("vector", "dv", lambda e, t=t: e.tensor_tensor(
            xs[:, t * D:(t + 1) * D], xs[:, t * D:(t + 1) * D], bb[:, :], ALU.add))
    p.wait("sync", "dv", wy)
    p.dma("sync", yo.rearrange("(a p) d -> p a d", p=128),
          xs[:, :].rearrange("p (a d) -> p a d", a=NT), "st")
    p.wait("sync", "st", p.cnt["st"])
    for nm in ("xin", "uzl", "rows", "rowsc", "wow", "w8w", "tp"):
        p.wait("sync", nm, p.cnt[nm])

    with p.es, nc.Block() as block:
        p.run(block)
    return nc


def kernel(L0_attn_x, L0_attn_Wq, L0_attn_bq, L0_attn_Wk, L0_attn_bk,
           L0_attn_Wv, L0_attn_bv, L0_attn_Wo, L0_attn_bo,
           L0_attn_ln_g, L0_attn_ln_b, L0_ffn_w_int8, L0_ffn_weight_scale,
           L0_ffn_bias):
    x = np.ascontiguousarray(np.asarray(L0_attn_x, np.float32).reshape(SEQ, D))
    T = SEQ // NCORES
    row = lambda a: np.ascontiguousarray(np.asarray(a, np.float32).reshape(1, -1))
    ws = np.ascontiguousarray(np.asarray(L0_ffn_weight_scale, np.float32).reshape(1, 1))
    w8 = np.ascontiguousarray(np.asarray(L0_ffn_w_int8, np.int32))
    com1 = {
        "Wq": np.ascontiguousarray(np.asarray(L0_attn_Wq, np.float32)),
        "Wk": np.ascontiguousarray(np.asarray(L0_attn_Wk, np.float32)),
        "Wv": np.ascontiguousarray(np.asarray(L0_attn_Wv, np.float32)),
        "bq": row(L0_attn_bq), "bk": row(L0_attn_bk), "bv": row(L0_attn_bv),
        "w8": w8, "ws": ws, "fb": row(L0_ffn_bias),
    }
    nc1 = build_l1(T)
    in1 = [dict(com1, x=np.ascontiguousarray(x[r * T:(r + 1) * T])) for r in range(NCORES)]
    res1 = run_bass_kernel_spmd(nc1, in1, core_ids=list(range(NCORES)))
    uzall = np.ascontiguousarray(
        np.concatenate([res1.results[r]["uz"] for r in range(NCORES)], axis=0))
    com2 = {
        "Wo": np.ascontiguousarray(np.asarray(L0_attn_Wo, np.float32)),
        "bo": row(L0_attn_bo), "g": row(L0_attn_ln_g), "b": row(L0_attn_ln_b),
        "w8": w8, "ws": ws, "fb": row(L0_ffn_bias), "uzall": uzall,
    }
    nc2 = build_l2(T)
    in2 = [dict(com2, x1=np.ascontiguousarray(res1.results[r]["x1"]),
                eqn=np.ascontiguousarray(res1.results[r]["eqn"]))
           for r in range(NCORES)]
    res2 = run_bass_kernel_spmd(nc2, in2, core_ids=list(range(NCORES)))
    y = np.concatenate([res2.results[r]["y"] for r in range(NCORES)], axis=0)
    return y.reshape(1, SEQ, D)




def emit_ln_stats_core(p, xs_t, stats_t, bnrow):
    p.op("vector", "dv", lambda e: e.bn_stats(bnrow[:, 0:6], xs_t[:, 0:512]))
    w2 = p.op("vector", "dv", lambda e: e.bn_stats(bnrow[:, 6:12], xs_t[:, 512:1024]))
    p.wait("vector", "dv", w2)
    w3 = p.op("vector", "dv", lambda e: e.bn_aggr(stats_t[:, 0:2], bnrow[:, :]))
    p.wait("vector", "dv", w3)
    return w3


def _sv(stats, nbt, c):
    return stats[:, :].rearrange("p (t k) -> p t k", k=8)[:, 0:nbt, c:c + 1]


def emit_batch_stats(p, xsrc, stats, bnrow, nbt, cores_done=False):
    scg = getattr(p, "stats_scg", None)
    if scg is None:
        scg = p.stats_scg = {}
    p.wait("vector", "dv", p.cnt.get("dv", 0))
    if not cores_done:
        p.wait("vector", "sc", scg.get(id(stats), 0))
        for t in range(nbt):
            emit_ln_stats_core(p, xsrc(t), stats[:, t * 8:(t + 1) * 8], bnrow)
    w1 = p.op("vector", "dv", lambda e: e.tensor_scalar(_sv(stats, nbt, 6), _sv(stats, nbt, 1), EPS_LN, None, ALU.add))
    p.wait("scalar", "dv", w1)
    w2 = p.op("scalar", "sc", lambda e: e.activation(_sv(stats, nbt, 6), _sv(stats, nbt, 6), AF.Ln))
    p.wait("scalar", "sc", w2)
    w3 = p.op("scalar", "sc", lambda e: e.activation(_sv(stats, nbt, 2), _sv(stats, nbt, 6), AF.Exp, scale=-0.5))
    p.wait("vector", "sc", w3)
    wn1 = p.op("vector", "dv", lambda e: e.tensor_tensor(_sv(stats, nbt, 3), _sv(stats, nbt, 0), _sv(stats, nbt, 2), ALU.mult))
    p.wait("vector", "dv", wn1)
    wn2 = p.op("vector", "dv", lambda e: e.tensor_scalar(_sv(stats, nbt, 3), _sv(stats, nbt, 3), -1.0, None, ALU.mult))
    p.wait("scalar", "dv", wn2)
    scg[id(stats)] = p.cnt["sc"]
    return wn2


def quant_stats_tile(p, xsrc, stats, bnrow, t):
    stt = stats[:, t * 8:(t + 1) * 8]
    emit_ln_stats_core(p, xsrc(t), stt, bnrow)
    p.op("vector", "dv", lambda e, t=t, stt=stt: e.tensor_reduce(stt[:, 3:4], xsrc(t)[:, :], AX, ALU.max))
    wmn = p.op("vector", "dv", lambda e, t=t, stt=stt: e.tensor_reduce(stt[:, 6:7], xsrc(t)[:, :], AX, ALU.min))
    p.wait("vector", "dv", wmn)
    p.op("vector", "dv", lambda e, stt=stt: e.tensor_tensor(stt[:, 3:4], stt[:, 3:4], stt[:, 0:1], ALU.subtract))
    w1 = p.op("vector", "dv", lambda e, stt=stt: e.tensor_tensor(stt[:, 6:7], stt[:, 0:1], stt[:, 6:7], ALU.subtract))
    p.wait("vector", "dv", w1)
    p.op("vector", "dv", lambda e, stt=stt: e.tensor_tensor(stt[:, 3:4], stt[:, 3:4], stt[:, 6:7], ALU.max))


def quant_math_batch(p, stats, nbt, wsb128):
    p.wait("vector", "dv", p.cnt.get("dv", 0))
    p.wait("scalar", "dv", p.cnt.get("dv", 0))
    w1 = p.op("vector", "dv", lambda e: e.tensor_scalar(_sv(stats, nbt, 2), _sv(stats, nbt, 1), EPS_LN, None, ALU.add))
    w1b = p.op("vector", "dv", lambda e: e.tensor_scalar(_sv(stats, nbt, 7), _sv(stats, nbt, 1), 1.0 + EPS_RMS, EPS_LN * EPS_RMS, ALU.mult, ALU.add))
    p.wait("scalar", "dv", w1b)
    wlna = p.op("scalar", "sc", lambda e: e.activation(_sv(stats, nbt, 2), _sv(stats, nbt, 2), AF.Ln))
    p.wait("scalar", "sc", wlna)
    p.op("scalar", "sc", lambda e: e.activation(_sv(stats, nbt, 2), _sv(stats, nbt, 2), AF.Exp, scale=-0.5))
    wlnb = p.op("scalar", "sc", lambda e: e.activation(_sv(stats, nbt, 7), _sv(stats, nbt, 7), AF.Ln))
    p.wait("scalar", "sc", wlnb)
    w8 = p.op("scalar", "sc", lambda e: e.activation(_sv(stats, nbt, 7), _sv(stats, nbt, 7), AF.Exp, scale=-0.5))
    p.wait("vector", "sc", w8)
    wd = p.op("vector", "dv", lambda e: e.tensor_tensor(_sv(stats, nbt, 3), _sv(stats, nbt, 3), _sv(stats, nbt, 7), ALU.mult))
    p.wait("vector", "dv", wd)
    we = p.op("vector", "dv", lambda e: e.tensor_scalar(_sv(stats, nbt, 3), _sv(stats, nbt, 3), EPS_ACT, None, ALU.max))
    p.wait("vector", "dv", we)
    wj = p.op("vector", "dv", lambda e: e.tensor_scalar(_sv(stats, nbt, 5), _sv(stats, nbt, 3), wsb128[:, 0:1], 0.5 / 127.0, ALU.mult, ALU.mult))
    p.wait("scalar", "dv", we)
    wf = p.op("scalar", "sc", lambda e: e.activation(_sv(stats, nbt, 4), _sv(stats, nbt, 3), AF.Ln))
    p.wait("scalar", "sc", wf)
    wg = p.op("scalar", "sc", lambda e: e.activation(_sv(stats, nbt, 4), _sv(stats, nbt, 4), AF.Exp, scale=-1.0))
    p.wait("vector", "sc", wg)
    wh = p.op("vector", "dv", lambda e: e.tensor_scalar(_sv(stats, nbt, 4), _sv(stats, nbt, 4), 127.0, None, ALU.mult))
    p.wait("vector", "dv", wh)
    wi_ = p.op("vector", "dv", lambda e: e.tensor_tensor(_sv(stats, nbt, 4), _sv(stats, nbt, 4), _sv(stats, nbt, 7), ALU.mult))
    p.wait("vector", "dv", wi_)
    wk_ = p.op("vector", "dv", lambda e: e.tensor_tensor(_sv(stats, nbt, 6), _sv(stats, nbt, 0), _sv(stats, nbt, 4), ALU.mult))
    p.wait("vector", "dv", wk_)
    wl_ = p.op("vector", "dv", lambda e: e.tensor_scalar(_sv(stats, nbt, 6), _sv(stats, nbt, 6), -1.0, None, ALU.mult))
    p.wait("scalar", "dv", wl_)


def quant_act_tile(p, xsrc, stats, t, itile, gt_t, war=()):
    it_war = p._it_war
    stt = stats[:, t * 8:(t + 1) * 8]
    it = itile[t % 2]
    gsem = f"gst{t % 2}"
    p.wait("scalar", gsem, it_war[t % 2])
    w17 = p.op("scalar", "sc", lambda e, t=t, stt=stt, it=it:
               e.activation(it[:, :], xsrc(t)[:, :], AF.Identity,
                            scale=stt[:, 4:5], bias=stt[:, 6:7]))
    p.wait("gpsimd", "sc", w17)
    for nm, v in war:
        p.wait("gpsimd", nm, v)
    p.dma("gpsimd", gt_t, it[:, :], gsem)
    it_war[t % 2] = p.cnt[gsem]


def emit_quant_batch(p, xsrc, stats, bnrow, ytile, itile, wsb128, gstage_t, nbt, war=(),
                     cores_done=False):
    it_war = getattr(p, "_it_war", [0, 0])
    p._it_war = it_war
    scg = p.stats_scg
    if not cores_done:
        p.wait("vector", "dv", p.cnt.get("dv", 0))
        p.wait("vector", "sc", scg.get(id(stats), 0))
        for t in range(nbt):
            quant_stats_tile(p, xsrc, stats, bnrow, t)
    quant_math_batch(p, stats, nbt, wsb128)
    for t in range(nbt):
        quant_act_tile(p, xsrc, stats, t, itile, gstage_t[t], war=war)
    p._q_sc = p.cnt["sc"]
    scg[id(stats)] = p.cnt["sc"]
    return [("gst0", p.cnt.get("gst0", 0)), ("gst1", p.cnt.get("gst1", 0))]


def transpose_batch(p, gstage, gT, bt, war=(), eng="sync", sem="gtp"):
    for nm, v in war:
        p.wait(eng, nm, v)
    for c in range(NCHUNK):
        p.dma(eng, gT[:, c * bt:(c + 1) * bt], gstage[:, c * 128:(c + 1) * 128], sem, transpose=True)
    return p.cnt[sem]


def mm_batch(p, mml, nbt, bt, lhs_buf, wt_buf, bias_row, evac, onesr):
    for t in range(nbt):
        mml.pre(t)
        rot = mml.rots[t % 2]
        for half in range(2):
            o0 = half * 512
            pb = rot[half]
            for c in range(NCHUNK):
                s, _ = p_bump(p, "pe")
                p.on("tensor", lambda e, t=t, c=c, o0=o0, pb=pb, s=s, first=(c == 0),
                     last=(c == NCHUNK - 1 and bias_row is None), lhs=lhs_buf, wt=wt_buf, bt=bt:
                     e.matmul(pb[:, :], lhs[:, c * bt + t * 128:c * bt + (t + 1) * 128],
                              wt[:, c * D + o0:c * D + o0 + 512],
                              start=first, stop=last).then_inc(s, 1))
            if bias_row is not None:
                s, _ = p_bump(p, "pe")
                p.on("tensor", lambda e, o0=o0, pb=pb, s=s, br=bias_row:
                     e.matmul(pb[:, :], onesr[:, :], br[:, o0:o0 + 512],
                              start=False, stop=True).then_inc(s, 1))
        g = evac(t, rot, p.cnt["pe"])
        mml.set_guard(t, g[0], g[1])


def p_bump(p, name):
    return p.bump(name, 1)


def build_fused(T, BT, SEQ_):
    nc = bass.Bass("TRN2", target_bir_lowering=False, debug=False)
    NB = T // BT
    NG = SEQ_ // BT
    NBT = BT // 128
    NT = T // 128
    own_batch0 = None

    xin = nc.dram_tensor("x", [T, D], F32, kind="ExternalInput").ap()
    xfi = nc.dram_tensor("xf", [SEQ_, D], F32, kind="ExternalInput").ap()
    Wq = nc.dram_tensor("Wq", [D, D], F32, kind="ExternalInput").ap()
    Wk = nc.dram_tensor("Wk", [D, D], F32, kind="ExternalInput").ap()
    Wv = nc.dram_tensor("Wv", [D, D], F32, kind="ExternalInput").ap()
    Wo = nc.dram_tensor("Wo", [D, D], F32, kind="ExternalInput").ap()
    bqi = nc.dram_tensor("bq", [1, D], F32, kind="ExternalInput").ap()
    bki = nc.dram_tensor("bk", [1, D], F32, kind="ExternalInput").ap()
    bvi = nc.dram_tensor("bv", [1, D], F32, kind="ExternalInput").ap()
    boi = nc.dram_tensor("bo", [1, D], F32, kind="ExternalInput").ap()
    gi = nc.dram_tensor("g", [1, D], F32, kind="ExternalInput").ap()
    bi = nc.dram_tensor("b", [1, D], F32, kind="ExternalInput").ap()
    w8i = nc.dram_tensor("w8", [D, D], I32, kind="ExternalInput").ap()
    wsi = nc.dram_tensor("ws", [1, 1], F32, kind="ExternalInput").ap()
    fbi = nc.dram_tensor("fb", [1, D], F32, kind="ExternalInput").ap()
    yo = nc.dram_tensor("y", [T, D], F32, kind="ExternalOutput").ap()
    stage = nc.dram_tensor("stage", [T, D], BF16).ap()
    eqnstage = nc.dram_tensor("eqnstage", [T, D], BF16).ap()
    gstage = nc.dram_tensor("gstage", [BT, D], BF16).ap()
    gstage2b = nc.dram_tensor("gstage2b", [BT, D], BF16).ap()
    lstage0 = nc.dram_tensor("lstage0", [BT, D], BF16).ap()
    lstage1 = nc.dram_tensor("lstage1", [BT, D], BF16).ap()
    wstage = nc.dram_tensor("wstage", [D, D], BF16).ap()
    wstage2 = nc.dram_tensor("wstage2", [D, D], BF16).ap()

    global p
    p = Prog(nc)
    xs = p.sb("xs", [128, NT * D])
    xga = p.sb("xga", [128, NBT * D], BF16)
    xgb = p.sb("xgb", [128, NBT * D], BF16)
    xgc = p.sb("xgc", [128, NBT * D], BF16)
    xgs = [xga, xgb, xgc]
    gTa = p.sb("gTa", [128, NCHUNK * BT], BF16)
    gTb = p.sb("gTb", [128, NCHUNK * BT], BF16)
    gTc = p.sb("gTc", [128, NCHUNK * BT], BF16)
    gTd = p.sb("gTd", [128, NCHUNK * BT], BF16)
    Wt_w8 = p.sb("Wt_w8", [128, NCHUNK * D], BF16)
    Wt_a = p.sb("Wt_a", [128, NCHUNK * D], BF16)
    Wt_b = p.sb("Wt_b", [128, NCHUNK * D], BF16)
    stats = p.sb("stats", [128, max(NT, NBT) * 8])
    statsU = [p.sb(f"statsU{i}", [128, NBT * 8]) for i in range(4)]
    bnrow = p.sb("bnrow", [128, 12])
    brow = p.sb("brow", [1, D], BF16)
    onesr = p.sb("onesr", [1, 128], BF16)
    ones32 = p.sb("ones32", [1, 128])
    wsb = p.sb("wsb", [1, 1])
    wsb128 = p.sb("wsb128", [128, 1])
    fbrow = p.sb("fbrow", [1, D], BF16)
    bvrow = p.sb("bvrow", [1, D], BF16)
    bvb = p.sb("bvb", [128, D], BF16)
    bb = p.sb("bb", [128, D], BF16)

    def bbv(o0=0, n=D):
        return bb[:, o0:o0 + n]
    ytile = None
    itile_t = p.sb("itile", [128, D], mybir.dt.int8)
    itile_t2 = p.sb("itile2", [128, D], mybir.dt.int8)
    f1a = p.sb("f1a", [128, D], BF16)
    f1b = p.sb("f1b", [128, D], BF16)
    ektile = p.sb("ektile", [128, D], BF16)
    vtile = p.sb("vtile", [128, 1040], BF16)
    eqtile = p.sb("eqtile", [128, D], BF16)
    atile = p.sb("atile", [128, D], BF16)
    xlntile = p.sb("xlntile", [128, D], BF16)
    xlntile2 = p.sb("xlntile2", [128, D], BF16)
    sr = p.sb("sr", [128, 16])
    uzsb = p.sb("uzsb", [128, 8 * 65], BF16)
    zr = p.sb("zr", [128, 8])
    state = p.sb("state", [128, 8 * 128], BF16)
    gF = p.sb("gF", [128, NCHUNK])
    bF = p.sb("bF", [128, NCHUNK], BF16)
    bF32 = p.sb("bF32", [128, NCHUNK])
    borow = p.sb("borow", [1, D], BF16)
    boeff = p.sb("boeff", [1, D], BF16)
    mm0 = p.ps("mm0", [128, 512])
    mm1 = p.ps("mm1", [128, 512])
    mm2 = p.ps("mm2", [128, 512])
    mm3 = p.ps("mm3", [128, 512])
    uzpe0 = p.ps("uzpe0", [64, 260])
    uzpo0 = p.ps("uzpo0", [64, 260])
    uzpe1 = p.ps("uzpe1", [64, 260])
    uzpo1 = p.ps("uzpo1", [64, 260])
    uzp = {(0, 0): uzpe0, (0, 1): uzpo0, (1, 0): uzpe1, (1, 1): uzpo1}
    mml = MMLoop(p, [(mm0, mm1), (mm2, mm3)])
    mml.onesr = onesr

    it2 = [itile_t[:, 0:D], itile_t2[:, 0:D]]
    xlts = [xlntile[:, 0:D], xlntile2[:, 0:D]]
    stage_t = stage.rearrange("(a p) d -> a p d", p=128)
    eqnstage_t = eqnstage.rearrange("(a p) d -> a p d", p=128)
    gstage_t = gstage.rearrange("(a p) d -> a p d", p=128)
    gstage2b_t = gstage2b.rearrange("(a p) d -> a p d", p=128)
    gsts = [gstage, gstage2b]
    gsts_t = [gstage_t, gstage2b_t]
    lsts = [lstage0, lstage1]
    lsts_t = [lstage0.rearrange("(a p) d -> a p d", p=128),
              lstage1.rearrange("(a p) d -> a p d", p=128)]
    gTs = [gTa, gTb, gTc, gTd]

    wx = p.dma("sync", xs[:, :].rearrange("p (a d) -> p a d", a=NT),
               xin.rearrange("(a p) d -> p a d", p=128), "xin")
    p.dma("gpsimd", brow[:, 0:D], bqi[:, :], "rowsc")
    p.dma("gpsimd", bvrow[:, :], bvi[:, :], "rowsc")
    p.dma("gpsimd", borow[:, :], boi[:, :], "rowsc")
    wbr = p.dma("gpsimd", fbrow[:, :], fbi[:, :], "rowsc")
    p.dma("sync", gF[:, :], gi.rearrange("o (c pp) -> pp (o c)", pp=128), "rows",
          allow_slow_non_contiguous=True)
    p.dma("sync", bF32[:, :], bi.rearrange("o (c pp) -> pp (o c)", pp=128), "rows",
          allow_slow_non_contiguous=True)
    wrows = p.dma("sync", wsb[:, :], wsi[:, :], "rows")
    p.wait("vector", "rows", wrows)
    p.wait("vector", "rowsc", wbr)
    wones = p.op("vector", "dv", lambda e: e.memset(onesr[:, :], 1.0))
    wones32 = p.op("vector", "dv", lambda e: e.memset(ones32[:, :], 1.0))
    wbF = p.op("vector", "dv", lambda e: e.tensor_copy(bF[:, :], bF32[:, :]))
    wmemv = p.op("vector", "dv", lambda e: e.memset(vtile[:, :], 1.0))
    p.wait("tensor", "dv", wones32)
    s, wpe_ws = p.bump("pe", 1)
    p.on("tensor", lambda e, s=s: e.matmul(mm2[:, 0:1], ones32[:, :], wsb[:, :], start=True, stop=True).then_inc(s, 1))
    p.wait("vector", "pe", wpe_ws)
    wwsb = p.op("vector", "dv", lambda e: e.tensor_copy(wsb128[:, :], mm2[:, 0:1]))
    mml.set_guard(1, "dv", wwsb)
    p.wait("tensor", "dv", wones)
    p.wait("tensor", "rowsc", wbr)
    s, _ = p.bump("pe", 1)
    p.on("tensor", lambda e, s=s: e.matmul(mm0[:, :], onesr[:, :], fbrow[:, 0:512], start=True, stop=True).then_inc(s, 1))
    s, wpe_bb = p.bump("pe", 1)
    p.on("tensor", lambda e, s=s: e.matmul(mm1[:, :], onesr[:, :], fbrow[:, 512:1024], start=True, stop=True).then_inc(s, 1))
    p.wait("vector", "pe", wpe_bb)
    p.op("vector", "dv", lambda e: e.tensor_scalar(bb[:, 0:512], mm0[:, :], 0.5, None, ALU.mult))
    wbb = p.op("vector", "dv", lambda e: e.tensor_scalar(bb[:, 512:1024], mm1[:, :], 0.5, None, ALU.mult))
    p.wait("tensor", "dv", wbb)
    s, _ = p.bump("pe", 1)
    p.on("tensor", lambda e, s=s: e.matmul(mm0[:, :], onesr[:, :], bvrow[:, 0:512], start=True, stop=True).then_inc(s, 1))
    s, wpe_bv = p.bump("pe", 1)
    p.on("tensor", lambda e, s=s: e.matmul(mm1[:, :], onesr[:, :], bvrow[:, 512:1024], start=True, stop=True).then_inc(s, 1))
    p.wait("vector", "pe", wpe_bv)
    p.op("vector", "dv", lambda e: e.tensor_copy(bvb[:, 0:512], mm0[:, :]))
    wbvb_dv = p.op("vector", "dv", lambda e: e.tensor_copy(bvb[:, 512:1024], mm1[:, :]))
    wbvb = wpe_bv
    mml.set_guard(0, "dv", wbvb_dv)
    p.wait("vector", "dv", max(wwsb, wbvb_dv))

    w_w8 = w_wk = w_wv = None

    def ffn_quant_part(xview, stage_dram_t, bidx, stage_war):
        st_ = statsU[bidx % 4]
        xb = lambda t, bidx=bidx: xview(bidx, t)
        marks = emit_quant_batch(p, xb, st_, bnrow, ytile, it2, wsb128,
                                 [stage_dram_t[bidx * NBT + t] for t in range(NBT)], NBT,
                                 war=list(stage_war))
        p.wait("vector", "sc", p._q_sc)
        for t in range(NBT):
            p.op("vector", "dv", lambda e, t=t, bidx=bidx:
                 e.tensor_tensor(xview(bidx, t)[:, :], xview(bidx, t)[:, :], bbv(), ALU.add))
        return (bidx, st_, marks, p.cnt["dv"])

    def ffn_pass(xview, stage_dram_t, stage_dram, nbatch, stage_war, gT_war, pre_q=None):
        mm_pe = [0, 0]
        qs = list(pre_q) if pre_q else []

        def getq(bidx):
            return qs[bidx] if bidx < len(qs) else ffn_quant_part(xview, stage_dram_t, bidx, stage_war)

        def transpose_part(bidx, st_, marks, wxbb):
            arena = gTs[bidx % 2]
            if bidx < 2:
                wg = gT_war[bidx % 2] if isinstance(gT_war, list) and gT_war and isinstance(gT_war[0], list) else list(gT_war)
            else:
                wg = [("pe", mm_pe[bidx % 2])]
            wtp = transpose_batch(p, stage_dram[bidx * BT:(bidx + 1) * BT, :] if nbatch > 1 else stage_dram[:, :],
                                  arena, BT, war=wg + marks)
            return (bidx, st_, arena, wtp, wxbb)

        def mm_stage(bidx, st_, arena, wtp, wxbb):
            p.wait("tensor", "gtp", wtp)
            p.wait("tensor", "w8w", w_w8)

            def evac(t, rot, wpe, bidx=bidx, wxbb=wxbb, st_=st_):
                stt_c = st_[:, t * 8 + 5:t * 8 + 6]
                p.wait("vector", "pe", wpe)
                p.wait("vector", "dv", wxbb)
                for half in range(2):
                    o0 = half * 512
                    wv = p.op("vector", "dv", lambda e, t=t, o0=o0, pb=rot[half], stt_c=stt_c, bidx=bidx:
                              e.scalar_tensor_tensor(xview(bidx, t)[:, o0:o0 + 512], pb[:, :], stt_c,
                                                     xview(bidx, t)[:, o0:o0 + 512], ALU.mult, ALU.add))
                return ("dv", wv)

            mm_batch(p, mml, NBT, BT, arena, Wt_w8, None, evac, onesr)
            mm_pe[bidx % 2] = p.cnt["pe"]

        cur = transpose_part(*getq(0))
        for bidx in range(nbatch):
            nq = transpose_part(*getq(bidx + 1)) if bidx + 1 < nbatch else None
            mm_stage(*cur)
            cur = nq
        return p.cnt["dv"], p.cnt["pe"]

    def it2_sel(bidx):
        return it2

    p.wait("vector", "xin", wx)
    p.wait("scalar", "xin", wx)
    xown = lambda b, t: xs[:, (b * NBT + t) * D:(b * NBT + t + 1) * D]

    uz_first = [True]
    pend_uz = [None]
    ekts = [ektile[:, 0:D], ektile[:, 0:D]]
    vts = [vtile[:, 0:1040], vtile[:, 0:1040]]
    xl_war = [0, 0]
    xlt_war = [0, 0]
    f1s = [f1a[:, 0:D], f1b[:, 0:D]]
    f1_war = [0, 0]
    t2_war = [0, 0]
    xg_guard = [(0, 0), (0, 0), (0, 0)]
    _pe0 = p.cnt["pe"]
    arena_guard = [_pe0, _pe0, _pe0, _pe0]
    kv_queue = []
    rotk_guard = [mml.guard[0]]
    rotv_guard = [mml.guard[1]]

    def stageA1load(g):
        xg = xgs[g % 3]
        xgv = lambda t, xg=xg: xg[:, t * D:(t + 1) * D]
        xsrc = (lambda t, g=g: xown(g, t)) if g < NB else xgv
        p.wait("gpsimd", "dv", xg_guard[g % 3][0])
        p.wait("gpsimd", "sc", xg_guard[g % 3][1])
        wxg = p.dma("gpsimd", xg[:, :].rearrange("p (a d) -> p a d", a=NBT),
                    xfi.rearrange("(a p) d -> p a d", p=128)[:, g * NBT:(g + 1) * NBT, :], "xgl")
        return (g, xgv, xsrc, wxg)

    def stats_preamble(g, xgv, xsrc, wxg):
        p.wait("vector", "xgl", p.cnt["xgl"])
        p.wait("vector", "sc", p.stats_scg.get(id(statsU[g % 4]), 0))

    def stageA1rest(g, xgv, xsrc, wxg, quant_done=False):
        par4 = g % 4
        gT = gTs[par4]
        gstg = gsts[g % 2]
        gstg_t = gsts_t[g % 2]
        t2sem = f"t2p{g % 2}"
        if quant_done:
            p._q_sc = p.cnt["sc"]
            p.stats_scg[id(statsU[par4])] = p.cnt["sc"]
            marks = [("gst0", p.cnt.get("gst0", 0)), ("gst1", p.cnt.get("gst1", 0))]
        else:
            p.wait("scalar", "xgl", p.cnt["xgl"])
            marks = emit_quant_batch(p, xsrc, statsU[par4], bnrow, ytile, it2, wsb128,
                                     [gstg_t[t] for t in range(NBT)], NBT,
                                     war=[(t2sem, t2_war[g % 2])], cores_done=True)
        wtp = transpose_batch(p, gstg, gT, BT, war=[("pe", arena_guard[par4])] + marks,
                              sem=t2sem)
        t2_war[g % 2] = p.cnt[t2sem]
        return (g, gT, xgv, wtp, p._q_sc, t2sem)

    def stageA2a(g, gT, xgv, wtp, q_sc_g, t2sem, nxt=None):
        p.wait("tensor", t2sem, wtp)
        p.wait("tensor", "w8w", w_w8)
        p.wait("vector", "sc", q_sc_g)
        for t in range(NBT):
            p.op("vector", "dv", lambda e, t=t, xgv=xgv: e.tensor_tensor(xgv(t)[:, :], xgv(t)[:, :], bbv(), ALU.add))
        wxbb_g = p.cnt["dv"]

        def evac_f(t, rot, wpe, wxbb_g=wxbb_g, xgv=xgv, g=g):
            stt_c = statsU[g % 4][:, t * 8 + 5:t * 8 + 6]
            f1 = f1s[t % 2]
            p.wait("scalar", "pe", wpe)
            p.wait("scalar", "dv", f1_war[t % 2])
            for half in range(2):
                o0 = half * 512
                wsc = p.op("scalar", "sc", lambda e, o0=o0, pb=rot[half], stt_c=stt_c, f1=f1:
                           e.activation(f1[:, o0:o0 + 512], pb[:, :], AF.Identity, scale=stt_c))
            guard = ("sc", wsc)
            p.wait("vector", "sc", wsc)
            p.wait("vector", "dv", wxbb_g)
            for half in range(2):
                o0 = half * 512
                wv = p.op("vector", "dv", lambda e, t=t, o0=o0, xgv=xgv, f1=f1:
                          e.tensor_tensor(xgv(t)[:, o0:o0 + 512], xgv(t)[:, o0:o0 + 512],
                                          f1[:, o0:o0 + 512], ALU.add))
            f1_war[t % 2] = wv
            if g < NB:
                xo = xown(g, t)
                for half in range(2):
                    o0 = half * 512
                    wv1 = p.op("vector", "dv", lambda e, o0=o0, xo=xo:
                               e.scalar_tensor_tensor(xo[:, o0:o0 + 512], bbv(o0, 512), 1.0,
                                                      xo[:, o0:o0 + 512], ALU.mult, ALU.add))
                    p.wait("vector", "dv", wv1)
                    wv2 = p.op("vector", "dv", lambda e, o0=o0, xo=xo, pb=rot[half], stt_c=stt_c:
                               e.scalar_tensor_tensor(xo[:, o0:o0 + 512], pb[:, :], stt_c,
                                                      xo[:, o0:o0 + 512], ALU.mult, ALU.add))
                guard = ("dv", wv2)
            if nxt is not None:
                if t == 0:
                    stats_preamble(*nxt)
                quant_stats_tile(p, nxt[2], statsU[nxt[0] % 4], bnrow, t)
            return guard

        mm_batch(p, mml, NBT, BT, gT, Wt_w8, None, evac_f, onesr)
        return (g, gT, xgv)

    def stageA2b(g, gT, xgv):
        gstg = lsts[g % 2]
        gstg_t = lsts_t[g % 2]
        emit_batch_stats(p, xgv, statsU[g % 4], bnrow, NBT)
        p.wait("scalar", "dv", p.cnt["dv"])
        for t in range(NBT):
            stt = statsU[g % 4][:, t * 8:(t + 1) * 8]
            xlt = xlts[t % 2]
            xsem = f"xst{t % 2}"
            p.wait("scalar", xsem, xlt_war[t % 2])
            w6 = p.op("scalar", "sc", lambda e, t=t, stt=stt, xlt=xlt, xgv=xgv:
                      e.activation(xlt[:, :], xgv(t)[:, :], AF.Identity,
                                   scale=stt[:, 2:3], bias=stt[:, 3:4]))
            p.wait("sync", "sc", w6)
            p.wait("sync", "gtp", p.cnt.get("gtp", 0))
            p.dma("sync", gstg_t[t], xlt[:, :], xsem)
            xlt_war[t % 2] = p.cnt[xsem]
        p.stats_scg[id(statsU[g % 4])] = p.cnt["sc"]
        xg_guard[g % 3] = (p.cnt["dv"], p.cnt["sc"])
        wtp2 = transpose_batch(p, gstg, gT, BT,
                               war=[("pe", p.cnt["pe"]),
                                    ("xst0", p.cnt.get("xst0", 0)),
                                    ("xst1", p.cnt.get("xst1", 0))])
        return (g, gT, wtp2)

    def emit_kv(g, gT, wtp2, filler=None):
        par4 = g % 4
        p.wait("tensor", "gtp", wtp2)
        p.wait("tensor", "wkw", w_wk)
        p.wait("tensor", "wvw", w_wv)
        p.wait("tensor", "rowsc", wbr)
        p.wait("tensor", "dv", wmemv)
        if mml.guard[0]:
            p.wait("tensor", mml.guard[0][0], mml.guard[0][1])
        guard1 = mml.guard[1]
        rotk, rotv = mml.rots[0], mml.rots[1]
        for t in range(NBT):
            if rotk_guard[0]:
                p.wait("tensor", rotk_guard[0][0], rotk_guard[0][1])
            for half in range(2):
                o0 = half * 512
                pb = rotk[half]
                for c in range(NCHUNK):
                    s, _ = p.bump("pe", 1)
                    p.on("tensor", lambda e, t=t, c=c, o0=o0, pb=pb, s=s, first=(c == 0), last=(c == NCHUNK - 1), gT=gT:
                         e.matmul(pb[:, :], gT[:, c * BT + t * 128:c * BT + (t + 1) * 128],
                                  Wt_a[:, c * D + o0:c * D + o0 + 512],
                                  start=first, stop=last).then_inc(s, 1))
            wpek = p.cnt["pe"]
            ekt = ekts[t % 2]
            p.wait("scalar", "pe", wpek)
            for half in range(2):
                o0 = half * 512
                wsc = p.op("scalar", "sc", lambda e, ekt=ekt, o0=o0, pb=rotk[half]:
                           e.activation(ekt[:, o0:o0 + 512], pb[:, :], AF.Exp))
            rotk_guard[0] = ("sc", wsc)
            if t == 0 and guard1:
                p.wait("tensor", guard1[0], guard1[1])
            if rotv_guard[0]:
                p.wait("tensor", rotv_guard[0][0], rotv_guard[0][1])
            for half in range(2):
                o0 = half * 512
                pb = rotv[half]
                for c in range(NCHUNK):
                    s, _ = p.bump("pe", 1)
                    p.on("tensor", lambda e, t=t, c=c, o0=o0, pb=pb, s=s, first=(c == 0), last=(c == NCHUNK - 1), gT=gT:
                         e.matmul(pb[:, :], gT[:, c * BT + t * 128:c * BT + (t + 1) * 128],
                                  Wt_b[:, c * D + o0:c * D + o0 + 512],
                                  start=first, stop=last).then_inc(s, 1))
            wpev = p.cnt["pe"]
            vt = vts[t % 2]
            p.wait("scalar", "pe", wpev)
            for half in range(2):
                o0 = half * 512
                wv = p.op("scalar", "sc", lambda e, vt=vt, o0=o0, pb=rotv[half]:
                          e.activation(
                              vt[:, :].rearrange("p (h x) -> p h x", x=65)[:, o0 // 64:o0 // 64 + 8, 0:64],
                              pb[:, :].rearrange("p (h x) -> p h x", x=64), AF.Copy))
            rotv_guard[0] = ("sc", wv)
            emit_uz(g, t, ekt, vt, wv)
            if filler is not None:
                filler(t)
        arena_guard[par4] = p.cnt["pe"]
        mml.set_guard(0, "sc", rotk_guard[0][1])
        mml.set_guard(1, "sc", rotv_guard[0][1])

    def emit_uz(g, t, ekt, vt, wv):
        p.wait("tensor", "sc", wv)
        for h in range(NH):
            j = h // 2
            pu = uzp[(j // 4, h % 2)]
            col = (j % 4) * 65
            first = uz_first[0] and j % 4 == 0
            last = (g == NG - 1) and (t == NBT - 1) and j % 4 == 3
            s, _ = p.bump("pe", 1)
            p.on("tensor", lambda e, h=h, pu=pu, col=col, s=s, first=first, last=last, ekt=ekt, vt=vt:
                 e.matmul(pu[:, col:col + 65],
                          ekt[:, h * 64:h * 64 + 64],
                          vt[:, h * 65:h * 65 + 65],
                          start=first, stop=last).then_inc(s, 1))
        uz_first[0] = False

    restq = []
    lgs = [stageA1load(0), stageA1load(1)]
    w_w8 = load_weight_F(p, w8i, wstage, Wt_w8, "w8w", eng="sync")
    for g0, lg in zip((0, 1), lgs):
        stats_preamble(*lg)
        for t in range(NBT):
            quant_stats_tile(p, lg[2], statsU[g0 % 4], bnrow, t)
        restq.append(stageA1rest(*lg))
    w_wk = load_weight_F(p, Wk, wstage2, Wt_a, "wkw", eng="sync")
    w_wv = load_weight_F(p, Wv, wstage, Wt_b, "wvw", cast_after=[("w8w", w_w8)], eng="sync")
    for g in range(NG):
        nxt = stageA1load(g + 2) if g + 2 < NG else None
        a2_cookie = stageA2a(*restq.pop(0), nxt=nxt)
        fil = None
        if len(kv_queue) >= 2 and nxt is not None:
            gn, _xgvn, xsrcn, wxgn = nxt
            gstg_tn = gsts_t[gn % 2]
            warn = [(f"t2p{gn % 2}", t2_war[gn % 2])]
            wxgt = p.cnt["xgl"]

            def fil(t, gn=gn, xsrcn=xsrcn, gstg_tn=gstg_tn, warn=warn, wxgt=wxgt):
                if t == 0:
                    p.wait("scalar", "xgl", wxgt)
                    quant_math_batch(p, statsU[gn % 4], NBT, wsb128)
                quant_act_tile(p, xsrcn, statsU[gn % 4], t, it2, gstg_tn[t], war=warn)
        if len(kv_queue) >= 2:
            emit_kv(*kv_queue.pop(0), filler=fil)
        elif nxt is not None:
            gn, _xgvn, xsrcn, wxgn = nxt
            p.wait("scalar", "xgl", p.cnt["xgl"])
            quant_math_batch(p, statsU[gn % 4], NBT, wsb128)
            for t in range(NBT):
                quant_act_tile(p, xsrcn, statsU[gn % 4], t, it2, gsts_t[gn % 2][t],
                               war=[(f"t2p{gn % 2}", t2_war[gn % 2])])
            fil = True
        kv_queue.append(stageA2b(*a2_cookie))
        if nxt is not None:
            restq.append(stageA1rest(*nxt, quant_done=(fil is not None)))
        wc = g - max(0, NG - 10)
        if 0 <= wc < NCHUNK:
            weight_cast_chunk(p, Wq, wstage2, "wqw", wc,
                              cast_after=([("wkw", w_wk)] if wc == 0 else ()))
            weight_cast_chunk(p, Wo, wstage, "wow", wc,
                              cast_after=([("wvw", w_wv)] if wc == 0 else ()))
    eq_war = [0, 0]
    d_pe = [0, 0]
    p.wait("sync", "t2p0", p.cnt.get("t2p0", 0))
    p.wait("sync", "t2p1", p.cnt.get("t2p1", 0))

    def d_feeder(bidx):
        st_ = statsU[bidx % 2]
        emit_batch_stats(p, lambda t, bidx=bidx: xown(bidx, t), st_, bnrow, NBT)
        p.wait("scalar", "dv", p.cnt["dv"])
        gstg = gsts[bidx % 2]
        gstg_t = gsts_t[bidx % 2]
        for t in range(NBT):
            stt = st_[:, t * 8:(t + 1) * 8]
            xlt = xlts[t % 2]
            xsem = f"xst{t % 2}"
            p.wait("scalar", xsem, xlt_war[t % 2])
            w6 = p.op("scalar", "sc", lambda e, t=t, bidx=bidx, stt=stt, xlt=xlt:
                      e.activation(xlt[:, :], xown(bidx, t)[:, :], AF.Identity,
                                   scale=stt[:, 2:3], bias=stt[:, 3:4]))
            p.wait("sync", "sc", w6)
            p.wait("sync", "gtp", p.cnt.get("gtp", 0))
            p.dma("sync", gstg_t[t], xlt[:, :], xsem)
            xlt_war[t % 2] = p.cnt[xsem]
        p.stats_scg[id(st_)] = p.cnt["sc"]
        arena = gTs[bidx % 2]
        war_pe = p.cnt["pe"] if bidx < 2 else d_pe[bidx % 2]
        wtp = transpose_batch(p, gstg, arena, BT,
                              war=[("pe", war_pe),
                                   ("xst0", p.cnt.get("xst0", 0)),
                                   ("xst1", p.cnt.get("xst1", 0))])
        return (bidx, arena, wtp)

    def d_mm(bidx, arena, wtp):
        p.wait("tensor", "gtp", wtp)
        p.wait("tensor", "wqw", w_wq)

        def evac_q(t, rot, wpe, bidx=bidx):
            p.wait("scalar", "pe", wpe)
            p.wait("scalar", "est")
            eqt = eqtile[:, 0:D]
            for half in range(2):
                o0 = half * 512
                wsc = p.op("scalar", "sc", lambda e, eqt=eqt, o0=o0, pb=rot[half]:
                           e.activation(eqt[:, o0:o0 + 512], pb[:, :], AF.Exp))
            p.wait("vector", "sc", wsc)
            w1 = p.op("vector", "dv", lambda e, eqt=eqt:
                      e.tensor_reduce(sr[:, 0:16], eqt[:, :].rearrange("p (h d) -> p h d", h=NH), AX, ALU.add))
            p.wait("vector", "dv", w1)
            w2 = p.op("vector", "dv", lambda e: e.reciprocal(sr[:, 0:16], sr[:, 0:16]))
            p.wait("vector", "dv", w2)
            w3 = p.op("vector", "dv", lambda e, eqt=eqt:
                      e.tensor_tensor(eqt[:, :].rearrange("p (h d) -> p h d", h=NH),
                                      eqt[:, :].rearrange("p (h d) -> p h d", h=NH),
                                      sr[:, 0:16].rearrange("p (h o) -> p h o", o=1).broadcast_to([128, NH, HD]),
                                      ALU.mult))
            p.wait("sync", "dv", w3)
            p.dma("sync", eqnstage_t[bidx * NBT + t], eqt[:, :], "est")
            eq_war[t % 2] = p.cnt["est"]
            return ("dv", w3)

        mm_batch(p, mml, NBT, BT, arena, Wt_a, brow[:, 0:D], evac_q, onesr)
        d_pe[bidx % 2] = p.cnt["pe"]

    dq = [d_feeder(b) for b in range(min(2, NB))]
    while kv_queue:
        emit_kv(*kv_queue.pop(0))
    if pend_uz[0] is not None:
        emit_uz(*pend_uz[0])
        pend_uz[0] = None
    wuz = p.cnt["pe"]

    p.wait("vector", "pe", wuz)
    p.op("vector", "dv", lambda e: e.tensor_copy(uzsb[0:64, 0:260], uzpe0[:, :]))
    p.op("vector", "dv", lambda e: e.tensor_copy(uzsb[64:128, 0:260], uzpo0[:, :]))
    p.op("vector", "dv", lambda e: e.tensor_copy(uzsb[0:64, 260:520], uzpe1[:, :]))
    wuze = p.op("vector", "dv", lambda e: e.tensor_copy(uzsb[64:128, 260:520], uzpo1[:, :]))
    p.wait("vector", "dv", wuze)
    p.wait("vector", "pe", wbvb)
    for j in range(8):
        h0, h1 = 2 * j, 2 * j + 1
        p.op("vector", "dv", lambda e, j=j, h0=h0: e.scalar_tensor_tensor(
            uzsb[0:64, j * 65:j * 65 + 64], bvb[0:64, h0 * 64:h0 * 64 + 64],
            uzsb[0:64, j * 65 + 64:j * 65 + 65], uzsb[0:64, j * 65:j * 65 + 64], ALU.mult, ALU.add))
        wbvfix = p.op("vector", "dv", lambda e, j=j, h1=h1: e.scalar_tensor_tensor(
            uzsb[64:128, j * 65:j * 65 + 64], bvb[64:128, h1 * 64:h1 * 64 + 64],
            uzsb[64:128, j * 65 + 64:j * 65 + 65], uzsb[64:128, j * 65:j * 65 + 64], ALU.mult, ALU.add))
    p.wait("vector", "dv", wbvfix)
    wzr = p.op("vector", "dv", lambda e: e.reciprocal(
        zr[:, :].rearrange("p (j o) -> p j o", o=1),
        uzsb[:, :].rearrange("p (j x) -> p j x", x=65)[:, :, 64:65]))
    p.wait("vector", "dv", wzr)
    wms = p.op("vector", "dv", lambda e: e.memset(state[:, :], 0.0))
    p.wait("vector", "dv", wms)
    for j in range(8):
        p.op("vector", "dv", lambda e, j=j: e.tensor_scalar(
            state[0:64, j * 128:j * 128 + 64], uzsb[0:64, j * 65:j * 65 + 64], zr[0:64, j:j + 1], None, ALU.mult))
        wstate = p.op("vector", "dv", lambda e, j=j: e.tensor_scalar(
            state[64:128, j * 128 + 64:j * 128 + 128], uzsb[64:128, j * 65:j * 65 + 64], zr[64:128, j:j + 1], None, ALU.mult))

    w_wq = weight_transposes(p, wstage2, Wt_a, "wqw", store_after=[("pe", wuz)])
    w_wo = weight_transposes(p, wstage, Wt_b, "wow", store_after=[("pe", wuz)])
    for bidx in range(NB):
        dnf = d_feeder(bidx + 2) if bidx + 2 < NB else None
        d_mm(*dq.pop(0))
        if dnf is not None:
            dq.append(dnf)
    west = p.cnt["est"]

    p.wait("tensor", "wow", w_wo)
    p.wait("tensor", "dv", wbF)
    mml.pre(0)
    for c in range(NCHUNK):
        s, _ = p.bump("pe", 1)
        p.on("tensor", lambda e, c=c, s=s: e.matmul(mm0[0:1, :], bF[:, c:c + 1], Wt_b[:, c * D:c * D + 512],
                                                    start=(c == 0), stop=(c == NCHUNK - 1)).then_inc(s, 1))
        s, wboe = p.bump("pe", 1)
        p.on("tensor", lambda e, c=c, s=s: e.matmul(mm1[0:1, :], bF[:, c:c + 1], Wt_b[:, c * D + 512:c * D + 1024],
                                                    start=(c == 0), stop=(c == NCHUNK - 1)).then_inc(s, 1))
    p.wait("vector", "pe", wboe)
    w1 = p.op("vector", "dv", lambda e: e.tensor_tensor(boeff[:, 0:512], mm0[0:1, :], borow[:, 0:512], ALU.add))
    wboeff = p.op("vector", "dv", lambda e: e.tensor_tensor(boeff[:, 512:1024], mm1[0:1, :], borow[:, 512:1024], ALU.add))
    mml.set_guard(0, "dv", wboeff)
    p.wait("vector", "pe", wboe)
    for c in range(NCHUNK):
        wgf = p.op("vector", "dv", lambda e, c=c: e.tensor_scalar(
            Wt_b[:, c * D:(c + 1) * D], Wt_b[:, c * D:(c + 1) * D], gF[:, c:c + 1], None, ALU.mult))
    wgfold = wgf

    p.wait("tensor", "dv", max(wstate, wgfold, wboeff))
    p.wait("vector", "sc", p.stats_scg.get(id(stats), 0))
    f_pe = [0, 0]

    def f_state(bidx):
        arena = gTs[bidx % 2]
        war_pe = p.cnt["pe"] if bidx < 2 else f_pe[bidx % 2]
        wtp = transpose_batch(p, eqnstage[bidx * BT:(bidx + 1) * BT, :] if NB > 1 else eqnstage[:, :],
                              arena, BT, war=[("pe", war_pe), ("est", west)])
        p.wait("tensor", "gtp", wtp)
        gstg = lsts[bidx % 2]
        gstg_t = lsts_t[bidx % 2]
        for t in range(NBT):
            mml.pre(t)
            rot = mml.rots[t % 2]
            for j in range(8):
                pb = rot[j // 4]
                col = (j * 128) % 512
                s, _ = p.bump("pe", 1)
                p.on("tensor", lambda e, t=t, j=j, pb=pb, col=col, s=s, arena=arena:
                     e.matmul(pb[:, col:col + 128],
                              arena[:, j * BT + t * 128:j * BT + (t + 1) * 128],
                              state[:, j * 128:(j + 1) * 128],
                              start=True, stop=True).then_inc(s, 1))
            wpe_a = p.cnt["pe"]
            at = atile[:, 0:D]
            p.wait("scalar", "pe", wpe_a)
            p.wait("scalar", "xst")
            for half in range(2):
                o0 = half * 512
                wsc = p.op("scalar", "sc", lambda e, at=at, o0=o0, pb=rot[half]:
                           e.activation(at[:, o0:o0 + 512], pb[:, :], AF.Copy))
            mml.set_guard(t, "sc", wsc)
            stt = stats[:, t * 8:(t + 1) * 8]
            p.wait("vector", "sc", wsc)
            emit_ln_stats_core(p, at, stt, bnrow)
            w3b = p.op("vector", "dv", lambda e, stt=stt: e.tensor_scalar(stt[:, 6:7], stt[:, 1:2], EPS_LN, None, ALU.add))
            p.wait("vector", "dv", w3b)
            p.wait("scalar", "dv", w3b)
            w4 = p.op("scalar", "sc", lambda e, stt=stt: e.activation(stt[:, 6:7], stt[:, 6:7], AF.Sqrt))
            p.wait("vector", "sc", w4)
            w5 = p.op("vector", "dv", lambda e, stt=stt: e.reciprocal(stt[:, 2:3], stt[:, 6:7]))
            p.wait("vector", "dv", w5)
            w6 = p.op("vector", "dv", lambda e, at=at, stt=stt:
                      e.tensor_scalar(at[:, :], at[:, :], stt[:, 0:1], stt[:, 2:3], ALU.subtract, ALU.mult))
            p.wait("sync", "dv", w6)
            p.wait("sync", "gtp", p.cnt.get("gtp", 0))
            p.dma("sync", gstg_t[t], at[:, :], "xst")
        arena2 = gTs[2 + bidx % 2]
        war_pe2 = p.cnt["pe"] if bidx < 2 else f_pe[bidx % 2]
        wtp2 = transpose_batch(p, gstg, arena2, BT,
                               war=[("pe", war_pe2), ("xst", p.cnt["xst"])])
        return (bidx, arena2, wtp2)

    def f_wo(bidx, arena2, wtp2):
        p.wait("tensor", "gtp", wtp2)
        p.wait("tensor", "wow", w_wo)

        def evac_wo(t, rot, wpe, bidx=bidx):
            p.wait("vector", "pe", wpe)
            for half in range(2):
                o0 = half * 512
                wv = p.op("vector", "dv", lambda e, t=t, o0=o0, pb=rot[half], bidx=bidx:
                          e.tensor_tensor(xown(bidx, t)[:, o0:o0 + 512], xown(bidx, t)[:, o0:o0 + 512],
                                          pb[:, :], ALU.add))
            return ("dv", wv)

        mm_batch(p, mml, NBT, BT, arena2, Wt_b, boeff, evac_wo, onesr)
        f_pe[bidx % 2] = p.cnt["pe"]

    gx = lambda b, t: xown(b, t)
    g_pre = []
    eqn_pe = [0, 0]
    fcur = f_state(0)
    eqn_pe[0] = p.cnt["pe"]
    for bidx in range(NB):
        fnx = f_state(bidx + 1) if bidx + 1 < NB else None
        eqn_pe[(bidx + 1) % 2] = p.cnt["pe"]
        f_wo(*fcur)
        g_pre.append(ffn_quant_part(gx, stage_t, bidx, []))
        fcur = fnx
    p.stats_scg[id(stats)] = p.cnt["sc"]

    ffn_pass(gx, stage_t, stage, NB,
             [("gtp", p.cnt.get("gtp", 0))],
             [[("pe", eqn_pe[0])], [("pe", eqn_pe[1])]], pre_q=g_pre)
    wy = p.cnt["dv"]
    p.wait("sync", "dv", wy)
    p.dma("sync", yo.rearrange("(a p) d -> p a d", p=128),
          xs[:, :].rearrange("p (a d) -> p a d", a=NT), "st")
    p.wait("sync", "st", p.cnt["st"])
    for nm in ("xin", "xgl", "rows", "rowsc", "w8w", "wkw", "wvw", "wqw", "wow",
               "gst0", "gst1", "xst0", "xst1", "xst", "gtp", "t2p0", "t2p1", "est"):
        if nm in p.cnt:
            p.wait("sync", nm, p.cnt[nm])

    with p.es, nc.Block() as block:
        p.run(block)
    return nc


FUSED = True


def kernel_fused(L0_attn_x, L0_attn_Wq, L0_attn_bq, L0_attn_Wk, L0_attn_bk,
                 L0_attn_Wv, L0_attn_bv, L0_attn_Wo, L0_attn_bo,
                 L0_attn_ln_g, L0_attn_ln_b, L0_ffn_w_int8, L0_ffn_weight_scale,
                 L0_ffn_bias):
    xf = np.ascontiguousarray(np.asarray(L0_attn_x, np.float32).reshape(SEQ, D))
    T = SEQ // NCORES
    BT = min(512, T)
    row = lambda a: np.ascontiguousarray(np.asarray(a, np.float32).reshape(1, -1))
    com = {
        "xf": xf,
        "Wq": np.ascontiguousarray(np.asarray(L0_attn_Wq, np.float32)),
        "Wk": np.ascontiguousarray(np.asarray(L0_attn_Wk, np.float32)),
        "Wv": np.ascontiguousarray(np.asarray(L0_attn_Wv, np.float32)),
        "Wo": np.ascontiguousarray(np.asarray(L0_attn_Wo, np.float32)),
        "bq": row(L0_attn_bq), "bk": row(L0_attn_bk), "bv": row(L0_attn_bv),
        "bo": row(L0_attn_bo), "g": row(L0_attn_ln_g), "b": row(L0_attn_ln_b),
        "w8": np.ascontiguousarray(np.asarray(L0_ffn_w_int8, np.int32)),
        "ws": np.ascontiguousarray(np.asarray(L0_ffn_weight_scale, np.float32).reshape(1, 1)),
        "fb": row(L0_ffn_bias),
    }
    nc = build_fused(T, BT, SEQ)
    im = [dict(com,
               x=np.ascontiguousarray(xf[r * T:(r + 1) * T]),
               xf=np.ascontiguousarray(np.roll(xf, -r * T, axis=0)))
          for r in range(NCORES)]
    res = run_bass_kernel_spmd(nc, im, core_ids=list(range(NCORES)))
    y = np.concatenate([res.results[r]["y"] for r in range(NCORES)], axis=0)
    return y.reshape(1, SEQ, D)


_kernel_2launch = kernel


def kernel(*args, **kw):
    if FUSED:
        return kernel_fused(*args, **kw)
    return _kernel_2launch(*args, **kw)
